# version 13
# baseline (speedup 1.0000x reference)
import numpy as np
from contextlib import ExitStack
import concourse.bass as bass
import concourse.mybir as mybir
from concourse.bass_utils import run_bass_kernel_spmd

F32 = mybir.dt.float32
BF16 = mybir.dt.bfloat16
I32 = mybir.dt.int32
AF = mybir.ActivationFunctionType
ALU = mybir.AluOpType
AX = mybir.AxisListType

D = 1024
S = 16
E = 4
FF = 4096
HID = 3072
NDI = 8
NKC = 8
FT = FF // 128
HT = HID // 128
DEQ = float(np.float32(1.0) / np.float32(127.0))
EPS = 1e-5

ENGINES = ("sync", "vector", "scalar", "tensor", "gpsimd")


class Sched:
    def __init__(self):
        self.ops = {e: [] for e in ENGINES}
        self.count = {}
        self.handles = {}
        self.guard = {e: {} for e in ENGINES}

    def sem(self, name):
        self.count[name] = 0

    def emit(self, eng, fn, inc=None, inc_n=1):
        if inc is not None:
            self.count[inc] += inc_n
            def wrapped(e, fn=fn, inc=inc, inc_n=inc_n):
                fn(e).then_inc(self.handles[inc], inc_n)
            self.ops[eng].append(wrapped)
            return self.count[inc]
        self.ops[eng].append(fn)
        return None

    def wait(self, eng, sem, val):
        if val is None or val <= 0:
            return
        if self.guard[eng].get(sem, 0) >= val:
            return
        self.guard[eng][sem] = val
        self.ops[eng].append(
            lambda e, sem=sem, val=val: e.wait_ge(self.handles[sem], val))

    def now(self, sem):
        return self.count[sem]


def build_program(L=8192, MC=512, TC=256):
    NMC = L // MC
    NSC = MC // TC
    NTT = MC // 128
    nc = bass.Bass("TRN2", target_bir_lowering=False)

    xT = nc.dram_tensor("xT", [D, L], F32, kind="ExternalInput").ap()
    A_log = nc.dram_tensor("A_log", [D, S], F32, kind="ExternalInput").ap()
    D_param = nc.dram_tensor("D_param", [D], F32, kind="ExternalInput").ap()
    Wd = nc.dram_tensor("Wd", [D, D], F32, kind="ExternalInput").ap()
    bd = nc.dram_tensor("bd", [D], F32, kind="ExternalInput").ap()
    WB = nc.dram_tensor("WB", [D, S], F32, kind="ExternalInput").ap()
    bB = nc.dram_tensor("bB", [S], F32, kind="ExternalInput").ap()
    WC = nc.dram_tensor("WC", [D, S], F32, kind="ExternalInput").ap()
    bC = nc.dram_tensor("bC", [S], F32, kind="ExternalInput").ap()
    Wr = nc.dram_tensor("Wr", [D, E], F32, kind="ExternalInput").ap()
    br = nc.dram_tensor("br", [E], F32, kind="ExternalInput").ap()
    up_w = nc.dram_tensor("up_w", [E, D, FF], F32, kind="ExternalInput").ap()
    up_b = nc.dram_tensor("up_b", [E, FF], F32, kind="ExternalInput").ap()
    down_w = nc.dram_tensor("down_w", [E, FF, D], F32, kind="ExternalInput").ap()
    down_b = nc.dram_tensor("down_b", [E, D], F32, kind="ExternalInput").ap()
    ln_w = nc.dram_tensor("ln_w", [D], F32, kind="ExternalInput").ap()
    ln_b = nc.dram_tensor("ln_b", [D], F32, kind="ExternalInput").ap()
    wgT = nc.dram_tensor("wgT", [D, HID], I32, kind="ExternalInput").ap()
    wuT = nc.dram_tensor("wuT", [D, HID], I32, kind="ExternalInput").ap()
    wdT = nc.dram_tensor("wdT", [HID, D], I32, kind="ExternalInput").ap()
    ident = nc.dram_tensor("ident", [128, 128], F32, kind="ExternalInput").ap()

    outT = nc.dram_tensor("outT", [D, L], F32, kind="ExternalOutput").ap()

    upbf = nc.dram_tensor("upbf", [E, FT, 128, NKC, 128], BF16, kind="Internal").ap()
    dnbf = nc.dram_tensor("dnbf", [E, FT, 128, D], BF16, kind="Internal").ap()
    wgubf = nc.dram_tensor("wgubf", [HT, 128, 2, NKC, 128], BF16, kind="Internal").ap()
    wdbf = nc.dram_tensor("wdbf", [HT, 128, D], BF16, kind="Internal").ap()

    sch = Sched()
    for s in ("qs", "qg", "qa", "qx", "qm", "v", "s", "p", "g",
              "su0", "su1", "sd0", "sd1", "sg0", "sg1", "sf0", "sf1"):
        sch.sem(s)

    def dma(eng, sem, dst, src, slow=False):
        def f(e, dst=dst, src=src, slow=slow):
            if slow:
                return e.dma_start(out=dst, in_=src, allow_slow_non_contiguous=True)
            return e.dma_start(out=dst, in_=src)
        return sch.emit(eng, f, inc=sem, inc_n=16)

    with ExitStack() as ctx:
        def sb(name, shape, dt):
            return ctx.enter_context(nc.sbuf_tensor(name, shape, dt))

        wd_sb = sb("wd_sb", [128, NKC, D], BF16)
        wb_sb = sb("wb_sb", [128, NKC, S], BF16)
        wc_sb = sb("wc_sb", [128, NKC, S], BF16)
        wr_sb = sb("wr_sb", [128, NKC, E], BF16)
        a_sb = sb("a_sb", [128, NDI * S], F32)
        d_sb = sb("d_sb", [128, NDI], F32)
        bd_sb = sb("bd_sb", [128, NDI], F32)
        lnw_sb = sb("lnw_sb", [128, NDI], F32)
        lnb_sb = sb("lnb_sb", [128, NDI], F32)
        dbb_sb = sb("dbb_sb", [128, E * NDI], F32)
        upb_sb = sb("upb_sb", [128, E * FT], F32)
        bB_sb = sb("bB_sb", [S, 1], F32)
        bC_sb = sb("bC_sb", [S, 1], F32)
        brbc_sb = sb("brbc_sb", [128, E], F32)
        ones_bf = sb("ones_bf", [128, 1], BF16)
        ones1_f = sb("ones1_f", [1, 128], F32)
        ones1_bf = sb("ones1_bf", [1, 128], BF16)
        nones1_f = sb("nones1_f", [1, 128], F32)
        ident_f = sb("ident_f", [128, 128], F32)
        hc_sb = sb("hc_sb", [128, NDI * S], F32)
        eps_sb = sb("eps_sb", [1, 1], F32)

        P = [ctx.enter_context(nc.psum_tensor(f"P{i}", [128, 512], F32)) for i in range(8)]
        for s in ("qs", "qg", "qa", "qx", "qm", "v", "s", "p", "g",
                  "su0", "su1", "sd0", "sd1", "sg0", "sg1", "sf0", "sf1"):
            sch.handles[s] = ctx.enter_context(nc.semaphore(s))

        bank_free = [("v", 0)] * 8

        def use_bank(i, eng="tensor"):
            sem, val = bank_free[i]
            sch.wait(eng, sem, val)

        def bank_consumed(i, sem, val):
            bank_free[i] = (sem, val)

        dma("gpsimd", "qm", a_sb[:, :].rearrange("p (di s) -> p di s", di=NDI), A_log.rearrange("(di p) s -> p di s", p=128), slow=True)
        dma("gpsimd", "qm", d_sb[:, :], D_param.rearrange("(di p) -> p di", p=128), slow=True)
        dma("gpsimd", "qm", bd_sb[:, :], bd.rearrange("(di p) -> p di", p=128), slow=True)
        dma("gpsimd", "qm", lnw_sb[:, :], ln_w.rearrange("(di p) -> p di", p=128), slow=True)
        dma("gpsimd", "qm", lnb_sb[:, :], ln_b.rearrange("(di p) -> p di", p=128), slow=True)
        dma("gpsimd", "qm", dbb_sb[:, :].rearrange("p (e di) -> p e di", e=E), down_b.rearrange("e (di p) -> p e di", p=128), slow=True)
        dma("gpsimd", "qm", bB_sb[:, 0:1], bB.rearrange("(s one) -> s one", one=1))
        dma("gpsimd", "qm", bC_sb[:, 0:1], bC.rearrange("(s one) -> s one", one=1))
        dma("gpsimd", "qm", brbc_sb[0:1, :], br.rearrange("(one e) -> one e", one=1))
        dma("gpsimd", "qm", ident_f[:, :], ident)

        sch.ops["gpsimd"].append(lambda e, v=sch.now("qm"): e.wait_ge(sch.handles["qm"], v))
        sch.count["g"] += 1
        sch.ops["gpsimd"].append(lambda e: e.sem_inc(sch.handles["g"], 1))
        smalls_bar = sch.now("g")
        small_ld = sch.now("qm")

        def init_consts(v):
            v.memset(ones_bf[:, :], 1.0)
            v.memset(ones1_f[:, :], 1.0)
            v.memset(ones1_bf[:, :], 1.0)
            v.memset(nones1_f[:, :], -1.0)
            v.memset(eps_sb[:, :], EPS)
            return v.memset(hc_sb[:, :], 0.0)
        cv = sch.emit("vector", init_consts, inc="v")

        sch.wait("scalar", "g", smalls_bar)
        s_a = sch.emit("scalar", lambda e: e.activation(a_sb[:, :], a_sb[:, :], AF.Exp), inc="s")
        sch.wait("vector", "s", s_a)
        sch.wait("vector", "g", smalls_bar)
        cv = sch.emit("vector", lambda e: e.tensor_scalar(a_sb[:, :], a_sb[:, :], -1.0, None, ALU.mult), inc="v")

        use_bank(0)
        sch.wait("tensor", "g", smalls_bar)
        sch.wait("tensor", "v", cv)
        pbr = sch.emit("tensor", lambda e: e.matmul(P[0][:, 0:E], ones1_f[0:1, :], brbc_sb[0:1, 0:E], start=True, stop=True), inc="p")
        sch.wait("vector", "p", pbr)
        cv = sch.emit("vector", lambda e: e.tensor_copy(brbc_sb[:, :], P[0][:, 0:E]), inc="v")
        bank_consumed(0, "v", cv)

        sxb = [sb(f"sxb{i}", [128, NDI, MC], BF16) for i in range(2)]
        dlt = sb("dlt", [128, NDI, MC], BF16)
        ysb = [sb(f"ysb{i}", [128, NDI, MC], BF16) for i in range(2)]
        acc = sb("acc", [128, NDI, MC], F32)
        hid = sb("hid", [128, FT, MC], BF16)
        mbc = sb("mbc", [128, E, MC], BF16)
        Brep = sb("Brep", [128, S, TC], BF16)
        Crep = sb("Crep", [128, S, TC], BF16)
        vwz = sb("vwz", [128, S, TC], BF16)
        zb = sb("zb", [128, S * TC], BF16)
        hh = sb("hh", [128, S, TC], BF16)
        bar = sb("bar", [128, 2, TC], F32)
        lnx = sb("lnx", [128, NDI, MC], BF16)
        bmc = sb("bmc", [S, 2, MC], BF16)
        bmf = sb("bmf", [1, 8, TC], BF16)
        mrow = sb("mrow", [4, MC], BF16)
        mflat = sb("mflat", [1, 2, MC], BF16)
        stat = sb("stat", [1, 2, MC], F32)
        rstat = sb("rstat", [128, NTT, 16], F32)
        tmp_f = sb("tmp_f", [128, MC], F32)
        tsil = sb("tsil", [128, MC], F32)
        ub = [sb(f"ub{i}", [128, 2, NKC, 128], BF16) for i in range(2)]
        db = [sb(f"db{i}", [128, D], BF16) for i in range(2)]
        gob = [sb(f"gob{i}", [128, 2, NKC, 128], BF16) for i in range(2)]
        fdb = [sb(f"fdb{i}", [128, D], BF16) for i in range(2)]
        xdt = sb("xdt", [128, TC], F32)

        zb_s = zb[:, :].rearrange("p (s t) -> p s t", s=S)
        sq_v = zb[:, 0:NDI * MC].rearrange("p (di t) -> p di t", di=NDI)


        hidflat = hid[:, :, :].rearrange("p a b -> p (a b)")
        hidf = hidflat.bitcast(F32)
        cstA = hidf[:, 0:2048]
        cstB = hidf[:, 2048:4096]
        cstiA = cstA.bitcast(I32)
        cstiB = cstB.bitcast(I32)
        cs2A = hidf[:, 4096:5120].bitcast(BF16)
        cs2B = hidf[:, 5120:6144].bitcast(BF16)
        stg_sb = hidf[:, 6144:6912]

        dma("gpsimd", "qm", stg_sb[:, 0:NKC * S].rearrange("p (kc s) -> p kc s", kc=NKC), WB.rearrange("(kc p) s -> p kc s", p=128), slow=True)
        dma("gpsimd", "qm", stg_sb[:, 256:256 + NKC * S].rearrange("p (kc s) -> p kc s", kc=NKC), WC.rearrange("(kc p) s -> p kc s", p=128), slow=True)
        stg_ld = dma("gpsimd", "qm", stg_sb[:, 512:512 + NKC * E].rearrange("p (kc e) -> p kc e", kc=NKC), Wr.rearrange("(kc p) e -> p kc e", p=128), slow=True)
        upb_ld = dma("gpsimd", "qm", stg_sb[:, 640:768], up_b.rearrange("e (q p) -> (e q) p", p=128))
        sch.ops["gpsimd"].append(lambda e, v=sch.now("qm"): e.wait_ge(sch.handles["qm"], v))
        sch.count["g"] += 1
        sch.ops["gpsimd"].append(lambda e: e.sem_inc(sch.handles["g"], 1))
        stg_bar = sch.now("g")
        use_bank(1)
        sch.wait("tensor", "g", stg_bar)
        tpv = sch.emit("tensor", lambda e: e.transpose(P[1][:, 0:128], stg_sb[:, 640:768], ident_f[:, :]), inc="p")
        sch.wait("vector", "p", tpv)
        ubv = sch.emit("vector", lambda e: e.tensor_copy(upb_sb[:, :], P[1][:, 0:128]), inc="v")
        bank_consumed(1, "v", ubv)
        sch.wait("vector", "g", stg_bar)
        sch.emit("vector", lambda e: e.tensor_copy(wb_sb[:, :, :], stg_sb[:, 0:NKC * S].rearrange("p (kc s) -> p kc s", kc=NKC)))
        sch.emit("vector", lambda e: e.tensor_copy(wc_sb[:, :, :], stg_sb[:, 256:256 + NKC * S].rearrange("p (kc s) -> p kc s", kc=NKC)))
        smalls_v = sch.emit("vector", lambda e: e.tensor_copy(wr_sb[:, :, :], stg_sb[:, 512:512 + NKC * E].rearrange("p (kc e) -> p kc e", kc=NKC)), inc="v")

        cast_hist = [0, 0]
        store_hist = [0, 0]
        conv_i = [0]

        def conv_step(load_dst_of, load_src, cast_fn_of, store_dst, store_src_of):
            i = conv_i[0]; conv_i[0] += 1
            par = i % 2
            sem = "su0" if par == 0 else "su1"
            cst = cstA if par == 0 else cstB
            csti = cstiA if par == 0 else cstiB
            cs2 = cs2A if par == 0 else cs2B
            sch.wait("sync", "g", cast_hist[par])
            lv = dma("sync", sem, load_dst_of(cst, csti), load_src)
            sch.wait("gpsimd", sem, lv)
            sch.wait("gpsimd", "qg", store_hist[par])
            cast_hist[par] = sch.emit("gpsimd", lambda e, cst=cst, csti=csti, cs2=cs2: cast_fn_of(e, cst, csti, cs2), inc="g")
            store_hist[par] = dma("gpsimd", "qg", store_dst, store_src_of(cs2))

        WdR = Wd.rearrange("(kc p) m -> p kc m", p=128)
        sch.wait("gpsimd", "v", smalls_v)
        wdcast = [0, 0]
        for kc in range(NKC):
            par = kc % 2
            sem = "su0" if par == 0 else "su1"
            cst = cstA if par == 0 else cstB
            sch.wait("sync", "g", wdcast[par])
            lv = dma("sync", sem, cst[:, 0:1024], WdR[:, kc, 0:1024])
            sch.wait("gpsimd", sem, lv)
            wdcast[par] = sch.emit("gpsimd", lambda e, kc=kc, cst=cst: e.tensor_copy(wd_sb[:, kc, :], cst[:, 0:1024]), inc="g")
        wd_conv_g = max(wdcast)
        cast_hist = [wd_conv_g, wd_conv_g]

        conv_up_mark = {}
        conv_dn_mark = {}
        upR = up_w.rearrange("e (kc p) f -> e p kc f", p=128)
        dnR = down_w.rearrange("e (ft p) d -> e ft p d", p=128)
        for ee in range(E):
            for f2 in range(0, FT, 2):
                conv_step(
                    lambda cst, csti: cst[:, :].rearrange("p (kc f) -> p kc f", kc=NKC),
                    upR[ee, :, :, f2 * 128:(f2 + 2) * 128],
                    lambda e, cst, csti, cs2: e.tensor_copy(
                        cs2.rearrange("p (i kc f) -> p i kc f", i=2, kc=NKC),
                        cst.rearrange("p (kc i f) -> p kc i f", kc=NKC, i=2).transpose([0, 2, 1, 3])),
                    upbf[ee, f2:f2 + 2, :, :, :].rearrange("c p kc f -> c p (kc f)").transpose([1, 0, 2]),
                    lambda cs2: cs2.rearrange("p (c v) -> p c v", c=2))
            sch.ops["gpsimd"].append(lambda e, v=sch.now("qg"): e.wait_ge(sch.handles["qg"], v))
            sch.count["g"] += 1
            sch.ops["gpsimd"].append(lambda e: e.sem_inc(sch.handles["g"], 1))
            conv_up_mark[ee] = sch.now("g")
            for f2 in range(0, FT, 2):
                conv_step(
                    lambda cst, csti: cst[:, :].rearrange("p (c d) -> p c d", c=2),
                    dnR[ee, f2:f2 + 2, :, :].transpose([1, 0, 2]),
                    lambda e, cst, csti, cs2: e.tensor_copy(cs2[:, :], cst[:, :]),
                    dnbf[ee, f2:f2 + 2, :, :].transpose([1, 0, 2]),
                    lambda cs2: cs2.rearrange("p (c d) -> p c d", c=2))
            sch.ops["gpsimd"].append(lambda e, v=sch.now("qg"): e.wait_ge(sch.handles["qg"], v))
            sch.count["g"] += 1
            sch.ops["gpsimd"].append(lambda e: e.sem_inc(sch.handles["g"], 1))
            conv_dn_mark[ee] = sch.now("g")
        HH = HT // 2
        for (slot, srcT) in ((0, wgT), (1, wuT)):
            srcR = srcT.rearrange("(kc p) h -> p kc h", p=128)
            for kc in range(NKC):
                for hh2 in range(2):
                    conv_step(
                        lambda cst, csti: csti[:, 0:HH * 128],
                        srcR[:, kc, hh2 * HH * 128:(hh2 + 1) * HH * 128],
                        lambda e, cst, csti, cs2: e.tensor_scalar(cs2[:, 0:HH * 128], csti[:, 0:HH * 128], DEQ, None, ALU.mult),
                        wgubf[hh2 * HH:(hh2 + 1) * HH, :, slot, kc, :].rearrange("ht p f -> p ht f"),
                        lambda cs2: cs2[:, 0:HH * 128].rearrange("p (ht f) -> p ht f", ht=HH))
        wdR = wdT.rearrange("(hc p) d -> hc p d", p=128)
        for h2 in range(0, HT, 2):
            conv_step(
                lambda cst, csti: csti[:, :].rearrange("p (c d) -> p c d", c=2),
                wdR[h2:h2 + 2, :, :].transpose([1, 0, 2]),
                lambda e, cst, csti, cs2: e.tensor_scalar(cs2[:, :], csti[:, :], DEQ, None, ALU.mult),
                wdbf[h2:h2 + 2, :, :].transpose([1, 0, 2]),
                lambda cs2: cs2.rearrange("p (c d) -> p c d", c=2))
        sch.ops["gpsimd"].append(lambda e, v=sch.now("qg"): e.wait_ge(sch.handles["qg"], v))
        sch.count["g"] += 1
        sch.ops["gpsimd"].append(lambda e: e.sem_inc(sch.handles["g"], 1))
        conv_done_qg = sch.now("g")
        conv_done_g = sch.now("g")

        xR = xT.rearrange("(di p) t -> p di t", p=128)
        oR = outT.rearrange("(di p) t -> p di t", p=128)

        stream_guard = {}
        pool_guard = {}
        bar_hist = []
        out_dma_mark = 0

        CX = lambda k: sxb[k % 2]
        CY = lambda k: ysb[k % 2]
        gather_mark = [0]
        rmask_mark = [0]
        rep_pv_mark = [0]
        mask_pv_mark = [0]
        moe_mark = {}
        masks_mark = {}
        bmc_mark = {}
        dlt_mark = {}
        ssm_mark = {}

        def emit_load_cast(k):
            nonlocal out_dma_mark
            sch.wait("scalar", "qa", out_dma_mark)
            lx = dma("scalar", "qx", acc[:, :, :], xR[:, :, k * MC:(k + 1) * MC])
            sch.wait("vector", "qx", lx)
            return sch.emit("vector", lambda e, k=k: e.tensor_copy(CX(k)[:, :, :], acc[:, :, :]), inc="v")

        def emit_acc_init(k):
            sch.wait("vector", "qa", out_dma_mark)
            return sch.emit("vector", lambda e, k=k: e.tensor_copy(acc[:, :, :], CY(k)[:, :, :]), inc="v")

        def emit_delta(k, vcast):
            cx = CX(k)
            sch.wait("tensor", "v", vcast)
            sch.wait("tensor", "g", wd_conv_g)
            sch.wait("scalar", "g", max(pool_guard.values()) if pool_guard else 0)
            last_sp_v = vcast
            for m in range(NDI):
                bk = m % 2
                use_bank(bk)
                for kc in range(NKC):
                    pv = sch.emit("tensor",
                                  lambda e, m=m, kc=kc, bk=bk, cx=cx: e.matmul(
                                      P[bk][:, :], wd_sb[:, kc, m * 128:(m + 1) * 128],
                                      cx[:, kc, :], start=(kc == 0), stop=(kc == NKC - 1)),
                                  inc="p")
                sch.wait("scalar", "p", pv)
                sch.wait("scalar", "v", last_sp_v)
                sch.emit("scalar", lambda e, m=m, bk=bk: e.activation(tmp_f[:, :], P[bk][:, :], AF.Abs, bias=bd_sb[:, m:m + 1], scale=1.0))
                sch.emit("scalar", lambda e: e.activation(tmp_f[:, :], tmp_f[:, :], AF.Exp, bias=0.0, scale=-1.0))
                sch.emit("scalar", lambda e: e.activation(tmp_f[:, :], tmp_f[:, :], AF.Ln, bias=1.0, scale=1.0), inc="s")
                sv_ = sch.emit("scalar", lambda e, m=m, bk=bk: e.activation(tsil[:, :], P[bk][:, :], AF.Relu, bias=bd_sb[:, m:m + 1], scale=1.0), inc="s")
                sch.wait("vector", "s", sv_)
                last_sp_v = sch.emit("vector", lambda e, m=m: e.tensor_tensor(dlt[:, m, :], tmp_f[:, :], tsil[:, :], ALU.add), inc="v")
                bank_consumed(bk, "s", sv_)
            dlt_mark[k] = last_sp_v
            for (col, wsb, bias_sb) in ((0, wb_sb, bB_sb), (1, wc_sb, bC_sb)):
                bk = 2 + col
                use_bank(bk)
                for kc in range(NKC):
                    pv = sch.emit("tensor",
                                  lambda e, wsb=wsb, kc=kc, bk=bk, cx=cx: e.matmul(
                                      P[bk][0:S, :], wsb[:, kc, 0:S], cx[:, kc, :],
                                      start=(kc == 0), stop=(kc == NKC - 1)),
                                  inc="p")
                sch.wait("vector", "p", pv)
                sch.wait("vector", "qa", gather_mark[0])
                vv = sch.emit("vector", lambda e, bk=bk, col=col, bias_sb=bias_sb: e.tensor_scalar(
                    bmc[0:S, col, :], P[bk][0:S, :], bias_sb[0:S, 0:1], None, ALU.add), inc="v")
                bank_consumed(bk, "v", vv)
            bmc_mark[k] = vv

        def bgstep(bg, n=1):
            if bg is None:
                return
            try:
                next(bg)
            except StopIteration:
                pass

        def bgfinish(bg):
            if bg is None:
                return
            for _ in bg:
                pass

        def ssm_gen(k, part):
            cx = CX(k); cy = CY(k)
            c2 = part // 2
            ct = c2 * TC
            if part % 2 == 0:
                for (col, rep) in ((0, Brep), (1, Crep)):
                    sch.wait("scalar", "v", bmc_mark[k])
                    sch.wait("scalar", "v", sch.now("v"))
                    for g2 in range(2):
                        sch.wait("scalar", "p", rep_pv_mark[0])
                        gv = dma("scalar", "qa", bmf[0:1, :, :],
                                 bmc[8 * g2:8 * g2 + 8, col, ct:ct + TC])
                        gather_mark[0] = gv
                        sch.wait("tensor", "qa", gv)
                        for jj in range(8):
                            j = 8 * g2 + jj
                            bk = 4 + (j % 2)
                            use_bank(bk)
                            pv = sch.emit("tensor", lambda e, jj=jj, bk=bk: e.matmul(
                                P[bk][:, 0:TC], ones1_bf[0:1, :], bmf[0:1, jj, :],
                                start=True, stop=True), inc="p")
                            sch.wait("scalar", "p", pv)
                            sv_ = sch.emit("scalar", lambda e, rep=rep, j=j, bk=bk: e.activation(
                                rep[:, j, :], P[bk][:, 0:TC], AF.Copy), inc="s")
                            bank_consumed(bk, "s", sv_)
                        rep_pv_mark[0] = pv
                        yield
            reps_done = sch.now("s")
            sch.wait("vector", "s", reps_done)
            sch.wait("scalar", "v", dlt_mark[k])
            for dii in range(4):
                di = (part % 2) * 4 + dii
                dsl = dlt[:, di:di + 1, ct:ct + TC].broadcast_to((128, S, TC))
                xsl = cx[:, di:di + 1, ct:ct + TC].broadcast_to((128, S, TC))
                sch.emit("vector", lambda e, dsl=dsl: e.tensor_tensor(vwz[:, :, :], dsl, Brep[:, :, :], ALU.mult))
                sch.emit("vector", lambda e: e.tensor_scalar(vwz[:, :, :], vwz[:, :, :], 2.0, -2.0, ALU.min, ALU.max))
                sch.emit("vector", lambda e, xsl=xsl: e.tensor_tensor(vwz[:, :, :], vwz[:, :, :], xsl, ALU.mult))
                yield
                for s_ in range(S):
                    gidx = len(bar_hist)
                    slot = gidx % 2
                    if gidx >= 2:
                        sch.wait("scalar", "v", bar_hist[gidx - 2])
                    sv_ = sch.emit("scalar", lambda e, di=di, ct=ct, s_=s_, slot=slot: e.activation(
                        bar[:, slot, :], dlt[:, di, ct:ct + TC], AF.Exp,
                        bias=0.0, scale=a_sb[:, di * S + s_:di * S + s_ + 1]), inc="s")
                    sch.wait("vector", "s", sv_)
                    first = (k == 0 and c2 == 0)
                    tv = sch.emit("vector", lambda e, di=di, s_=s_, slot=slot, first=first: e.tensor_tensor_scan(
                        hh[:, s_, :], bar[:, slot, :], vwz[:, s_, :],
                        (0.0 if first else hc_sb[:, di * S + s_:di * S + s_ + 1]),
                        ALU.mult, ALU.add), inc="v")
                    bar_hist.append(tv)
                    if s_ % 4 == 3:
                        yield
                sch.wait("vector", "v", bar_hist[-1])
                sch.emit("vector", lambda e, di=di: e.tensor_copy(
                    hc_sb[:, di * S:(di + 1) * S],
                    hh[:, :, TC - 1:TC].rearrange("p s one -> p (s one)")))
                zv = zb[:, 0:S * TC].rearrange("p (s t) -> p s t", s=S)
                sch.emit("vector", lambda e, zv=zv: e.tensor_tensor(zv[:, :, :], hh[:, :, :], Crep[:, :, :], ALU.mult))
                sch.emit("vector", lambda e, zv=zv: e.tensor_tensor(zv[:, 0:8, :], zv[:, 0:8, :], zv[:, 8:16, :], ALU.add))
                sch.emit("vector", lambda e, zv=zv: e.tensor_tensor(zv[:, 0:4, :], zv[:, 0:4, :], zv[:, 4:8, :], ALU.add))
                sch.emit("vector", lambda e, di=di, ct=ct, cx=cx: e.tensor_scalar(
                    xdt[:, :], cx[:, di, ct:ct + TC], d_sb[:, di:di + 1], None, ALU.mult))
                dmk = sch.emit("vector", lambda e, zv=zv: e.tensor_tensor(zv[:, 0:2, :], zv[:, 0:2, :], zv[:, 2:4, :], ALU.add), inc="v")
                sch.wait("vector", "v", dmk)
                sch.emit("vector", lambda e, zv=zv: e.tensor_tensor(zv[:, 0:1, :], zv[:, 0:1, :], zv[:, 1:2, :], ALU.add))
                vv = sch.emit("vector", lambda e, di=di, ct=ct, cy=cy, zv=zv: e.tensor_tensor(
                    cy[:, di, ct:ct + TC], zv[:, 0, :], xdt[:, :], ALU.add), inc="v")
                yield
            ssm_mark[k] = vv

        def emit_routing(k):
            cy = CY(k)
            use_bank(6)
            sch.wait("tensor", "v", ssm_mark[k])
            for tt in range(NTT):
                for kc in range(NKC):
                    pv = sch.emit("tensor", lambda e, tt=tt, kc=kc, cy=cy: e.matmul(
                        P[6][:, tt * 4:tt * 4 + 4], cy[:, kc, tt * 128:(tt + 1) * 128],
                        wr_sb[:, kc, 0:E], start=(kc == 0), stop=(kc == NKC - 1)), inc="p")
            sch.wait("vector", "p", pv)
            r = rstat
            for tt in range(NTT):
                sch.emit("vector", lambda e, tt=tt: e.tensor_tensor(
                    r[:, tt, 0:4], P[6][:, tt * 4:tt * 4 + 4], brbc_sb[:, :], ALU.add))
            for tt in range(NTT):
                mark = sch.emit("vector", lambda e, tt=tt: e.tensor_reduce(r[:, tt, 4:5], r[:, tt, 0:4], AX.X, ALU.max), inc="v")
            bank_consumed(6, "v", mark)
            sch.wait("vector", "v", mark)
            for tt in range(NTT):
                mark = sch.emit("vector", lambda e, tt=tt: e.tensor_scalar(r[:, tt, 5:6], r[:, tt, 4:5], -1.0, None, ALU.mult), inc="v")
            sch.wait("scalar", "v", mark)
            for tt in range(NTT):
                smark = sch.emit("scalar", lambda e, tt=tt: e.activation(
                    r[:, tt, 8:12], r[:, tt, 0:4], AF.Exp, bias=r[:, tt, 5:6], scale=1.0), inc="s")
            sch.wait("vector", "s", smark)
            for tt in range(NTT):
                mark = sch.emit("vector", lambda e, tt=tt: e.tensor_reduce(r[:, tt, 12:13], r[:, tt, 8:12], AX.X, ALU.add), inc="v")
            sch.wait("vector", "v", mark)
            for tt in range(NTT):
                sch.emit("vector", lambda e, tt=tt: e.reciprocal(r[:, tt, 13:14], r[:, tt, 12:13]))
            for tt in range(NTT):
                mark = sch.emit("vector", lambda e, tt=tt: e.tensor_scalar(r[:, tt, 0:4], r[:, tt, 0:4], r[:, tt, 4:5], None, ALU.is_equal), inc="v")
            sch.wait("vector", "v", mark)
            for tt in range(NTT):
                sch.emit("vector", lambda e, tt=tt: e.tensor_scalar(r[:, tt, 6:7], r[:, tt, 0:1], -1.0, 1.0, ALU.mult, ALU.add))
                sch.emit("vector", lambda e, tt=tt: e.tensor_scalar(r[:, tt, 7:8], r[:, tt, 1:2], -1.0, 1.0, ALU.mult, ALU.add))
                mark = sch.emit("vector", lambda e, tt=tt: e.tensor_scalar(r[:, tt, 14:15], r[:, tt, 2:3], -1.0, 1.0, ALU.mult, ALU.add), inc="v")
            sch.wait("vector", "v", mark)
            for tt in range(NTT):
                sch.emit("vector", lambda e, tt=tt: e.tensor_tensor(r[:, tt, 1:2], r[:, tt, 1:2], r[:, tt, 6:7], ALU.mult))
                mark = sch.emit("vector", lambda e, tt=tt: e.tensor_tensor(r[:, tt, 15:16], r[:, tt, 6:7], r[:, tt, 7:8], ALU.mult), inc="v")
            sch.wait("vector", "v", mark)
            for tt in range(NTT):
                sch.emit("vector", lambda e, tt=tt: e.tensor_tensor(r[:, tt, 2:3], r[:, tt, 2:3], r[:, tt, 15:16], ALU.mult))
                mark = sch.emit("vector", lambda e, tt=tt: e.tensor_tensor(r[:, tt, 15:16], r[:, tt, 15:16], r[:, tt, 14:15], ALU.mult), inc="v")
            sch.wait("vector", "v", mark)
            for tt in range(NTT):
                mark = sch.emit("vector", lambda e, tt=tt: e.tensor_tensor(r[:, tt, 3:4], r[:, tt, 3:4], r[:, tt, 15:16], ALU.mult), inc="v")
            sch.wait("vector", "v", mark)
            for tt in range(NTT):
                mark = sch.emit("vector", lambda e, tt=tt: e.tensor_scalar(r[:, tt, 0:4], r[:, tt, 0:4], r[:, tt, 13:14], None, ALU.mult), inc="v")
            rmask_mark[0] = mark

        def emit_masks(k):
            r = rstat
            mark = rmask_mark[0]
            use_bank(7)
            sch.wait("tensor", "v", mark)
            for tt in range(NTT):
                pv = sch.emit("tensor", lambda e, tt=tt: e.transpose(
                    P[7][0:4, tt * 128:(tt + 1) * 128], r[:, tt, 0:4], ident_f[:, :]), inc="p")
            sch.wait("vector", "p", pv)
            vv = sch.emit("vector", lambda e: e.tensor_copy(mrow[:, :], P[7][0:4, 0:MC]), inc="v")
            bank_consumed(7, "v", vv)
            sch.wait("scalar", "v", vv)
            sch.wait("scalar", "g", sch.now("g"))
            for g2 in range(2):
                sch.wait("scalar", "p", mask_pv_mark[0])
                gv = dma("scalar", "qa", mflat[0:1, :, :], mrow[2 * g2:2 * g2 + 2, :])
                sch.wait("tensor", "qa", gv)
                for jj in range(2):
                    ee = 2 * g2 + jj
                    bk = 4 + (ee % 2)
                    use_bank(bk)
                    pv = sch.emit("tensor", lambda e, jj=jj, bk=bk: e.matmul(
                        P[bk][:, :], ones1_bf[0:1, :], mflat[0:1, jj, :],
                        start=True, stop=True), inc="p")
                    sch.wait("scalar", "p", pv)
                    sv_ = sch.emit("scalar", lambda e, ee=ee, bk=bk: e.activation(
                        mbc[:, ee, :], P[bk][:, :], AF.Copy), inc="s")
                    bank_consumed(bk, "s", sv_)
                mask_pv_mark[0] = pv
            masks_mark[k] = sch.now("s")

        def emit_moe_expert(k, ee, dn_prev, bg=None, mid_cb=None):
            cy = CY(k)
            if k == 0:
                sch.wait("sync", "g", conv_up_mark[ee])
            bgstep(bg); bgstep(bg)
            sch.wait("scalar", "p", dn_prev)
            for j2 in range(FT // 2):
                buf = ub[j2 % 2]
                key = ("up", j2 % 2)
                sem = "su0" if j2 % 2 == 0 else "su1"
                sch.wait("sync", "p", stream_guard.get(key, 0))
                lv = dma("sync", sem, buf[:, :, :, :].rearrange("p c kc f -> p c (kc f)"),
                         upbf[ee, 2 * j2:2 * j2 + 2, :, :, :].rearrange("c p kc f -> p c (kc f)"))
                sch.wait("tensor", sem, lv)
                for jj in range(2):
                    ft2 = 2 * j2 + jj
                    bk = ft2 % 4
                    use_bank(bk)
                    for kc in range(NKC):
                        pv = sch.emit("tensor", lambda e, buf=buf, jj=jj, kc=kc, bk=bk, cy=cy: e.matmul(
                            P[bk][:, :], buf[:, jj, kc, :], cy[:, kc, :],
                            start=(kc == 0), stop=(kc == NKC - 1)), inc="p")
                    sch.wait("scalar", "p", pv)
                    sv_ = sch.emit("scalar", lambda e, ee=ee, ft2=ft2, bk=bk: e.activation(
                        hid[:, ft2, :], P[bk][:, :], AF.Silu,
                        bias=upb_sb[:, ee * FT + ft2:ee * FT + ft2 + 1], scale=1.0), inc="s")
                    bank_consumed(bk, "s", sv_)
                stream_guard[key] = pv
                bgstep(bg)
            hid_done = sch.now("s")
            if mid_cb is not None:
                mid_cb()
            if k == 0:
                sch.wait("sync", "g", conv_dn_mark[ee])
            sch.wait("tensor", "s", hid_done)
            for bk in range(8):
                use_bank(bk)
            for fc in range(FT):
                buf = db[fc % 2]
                key = ("dn", fc % 2)
                sch.wait("sync", "p", stream_guard.get(key, 0))
                lv = dma("sync", "qs", buf[:, :], dnbf[ee, fc, :, :])
                sch.wait("tensor", "qs", lv)
                for dt in range(8):
                    pv = sch.emit("tensor", lambda e, buf=buf, dt=dt, fc=fc: e.matmul(
                        P[dt][:, :], buf[:, dt * 128:(dt + 1) * 128], hid[:, fc, :],
                        start=(fc == 0), stop=(fc == FT - 1)), inc="p")
                stream_guard[key] = pv
                if fc % 4 == 3:
                    bgstep(bg)
            dn_pv = pv
            sch.wait("vector", "p", dn_pv)
            sch.wait("vector", "s", masks_mark[k])
            sch.wait("gpsimd", "s", masks_mark[k])
            for dt in range(8):
                t3 = tmp_f if dt % 2 == 0 else tsil
                sch.wait("vector", "g", pool_guard.get(dt % 2, 0))
                vv_ = sch.emit("vector", lambda e, ee=ee, dt=dt, t3=t3: e.scalar_tensor_tensor(
                    t3[:, :], P[dt][:, :], dbb_sb[:, ee * NDI + dt:ee * NDI + dt + 1],
                    mbc[:, ee, :], ALU.add, ALU.mult), inc="v")
                bank_consumed(dt, "v", vv_)
                sch.wait("gpsimd", "v", vv_)
                gv_ = sch.emit("gpsimd", lambda e, dt=dt, t3=t3: e.tensor_tensor(
                    acc[:, dt, :], acc[:, dt, :], t3[:, :], ALU.add), inc="g")
                pool_guard[dt % 2] = gv_
            return dn_pv

        def emit_ln(stats_src_f32, sq_wait_v):
            svv = sch.now("v")
            if stats_src_f32 is not None:
                svv = sch.emit("vector", lambda e: e.tensor_copy(lnx[:, :, :], stats_src_f32[:, :, :]), inc="v")
            sch.wait("scalar", "v", max(svv, sq_wait_v))
            ssv = sch.emit("scalar", lambda e: e.activation(sq_v[:, :, :], lnx[:, :, :], AF.Square), inc="s")
            use_bank(6); use_bank(7)
            sch.wait("tensor", "v", svv)
            sch.wait("tensor", "s", ssv)
            for kc in range(NKC):
                sch.emit("tensor", lambda e, kc=kc: e.matmul(
                    P[6][0:1, :], ones_bf[:, 0:1], lnx[:, kc, :],
                    start=(kc == 0), stop=(kc == NKC - 1)), inc="p")
            for kc in range(NKC):
                pv = sch.emit("tensor", lambda e, kc=kc: e.matmul(
                    P[7][0:1, :], ones_bf[:, 0:1], sq_v[:, kc, :],
                    start=(kc == 0), stop=(kc == NKC - 1)), inc="p")
            sch.wait("vector", "p", pv)
            sch.emit("vector", lambda e: e.tensor_scalar(P[6][0:1, :], P[6][0:1, :], 1.0 / D, None, ALU.mult))
            vv = sch.emit("vector", lambda e: e.tensor_scalar(P[7][0:1, :], P[7][0:1, :], 1.0 / D, None, ALU.mult), inc="v")
            sch.wait("scalar", "v", vv)
            sv2 = sch.emit("scalar", lambda e: e.activation(stat[0:1, 0, :], P[6][0:1, :], AF.Square), inc="s")
            sch.wait("vector", "s", sv2)
            vv = sch.emit("vector", lambda e: e.tensor_tensor(P[7][0:1, :], P[7][0:1, :], stat[0:1, 0, :], ALU.subtract), inc="v")
            sch.wait("scalar", "v", vv)
            sch.emit("scalar", lambda e: e.activation(stat[0:1, 0, :], P[7][0:1, :], AF.Ln, bias=eps_sb[0:1, 0:1], scale=1.0))
            sv2 = sch.emit("scalar", lambda e: e.activation(stat[0:1, 0, :], stat[0:1, 0, :], AF.Exp, bias=0.0, scale=-0.5), inc="s")
            sch.wait("vector", "s", sv2)
            vv = sch.emit("vector", lambda e: e.tensor_tensor(stat[0:1, 1, :], P[6][0:1, :], stat[0:1, 0, :], ALU.mult), inc="v")
            bank_consumed(6, "v", vv); bank_consumed(7, "v", vv)
            use_bank(4); use_bank(5)
            sch.wait("tensor", "v", vv)
            sch.emit("tensor", lambda e: e.matmul(P[4][:, :], ones1_f[0:1, :], stat[0:1, 0, :], start=True, stop=True), inc="p")
            pv = sch.emit("tensor", lambda e: e.matmul(P[5][:, :], nones1_f[0:1, :], stat[0:1, 1, :], start=True, stop=True), inc="p")
            sch.wait("vector", "p", pv)
            return pv

        def emit_lns(k):
            cx = CX(k); cy = CY(k)
            sch.wait("vector", "g", moe_mark[k])
            sch.wait("scalar", "g", moe_mark[k])
            emit_ln(acc, ssm_mark.get(k + 1, 0))
            for di in range(NDI):
                sch.emit("vector", lambda e, di=di: e.tensor_tensor(tmp_f[:, :], acc[:, di, :], P[4][:, :], ALU.mult))
                sch.emit("vector", lambda e, di=di: e.tensor_tensor(tmp_f[:, :], tmp_f[:, :], P[5][:, :], ALU.add))
                sch.emit("vector", lambda e, di=di, cx=cx: e.scalar_tensor_tensor(
                    tmp_f[:, :], tmp_f[:, :], lnw_sb[:, di:di + 1], cx[:, di, :], ALU.mult, ALU.add))
                vv = sch.emit("vector", lambda e, di=di: e.tensor_scalar(
                    lnx[:, di, :], tmp_f[:, :], lnb_sb[:, di:di + 1], None, ALU.add), inc="v")
            bank_consumed(4, "v", vv); bank_consumed(5, "v", vv)
            emit_ln(None, ssm_mark.get(k + 1, 0))
            for di in range(NDI):
                sch.emit("vector", lambda e, di=di: e.tensor_tensor(tmp_f[:, :], lnx[:, di, :], P[4][:, :], ALU.mult))
                sch.emit("vector", lambda e, di=di: e.tensor_tensor(acc[:, di, :], tmp_f[:, :], P[5][:, :], ALU.add))
                vv = sch.emit("vector", lambda e, di=di, cy=cy: e.tensor_copy(cy[:, di, :], acc[:, di, :]), inc="v")
            bank_consumed(4, "v", vv); bank_consumed(5, "v", vv)
            return vv

        def emit_ffn(k, x1_done):
            cy = CY(k)
            sch.wait("tensor", "v", x1_done)
            last_sil_v = x1_done
            for ht2 in range(HT):
                buf = gob[ht2 % 2]
                key = ("ffn", ht2 % 2)
                if k == 0:
                    sch.wait("sync", "g", conv_done_qg)
                sem = "sg0" if ht2 % 2 == 0 else "sg1"
                sch.wait("sync", "p", stream_guard.get(key, 0))
                lv = dma("sync", sem, buf[:, :, :, :].rearrange("p c kc f -> p c (kc f)"),
                         wgubf[ht2, :, :, :, :].rearrange("p c kc f -> p c (kc f)"))
                bka = (ht2 % 2) * 2
                bkb = bka + 1
                use_bank(bka); use_bank(bkb)
                sch.wait("tensor", sem, lv)
                for kc in range(NKC):
                    sch.emit("tensor", lambda e, buf=buf, kc=kc, bka=bka, cy=cy: e.matmul(
                        P[bka][:, :], buf[:, 0, kc, :], cy[:, kc, :],
                        start=(kc == 0), stop=(kc == NKC - 1)), inc="p")
                for kc in range(NKC):
                    pv = sch.emit("tensor", lambda e, buf=buf, kc=kc, bkb=bkb, cy=cy: e.matmul(
                        P[bkb][:, :], buf[:, 1, kc, :], cy[:, kc, :],
                        start=(kc == 0), stop=(kc == NKC - 1)), inc="p")
                stream_guard[key] = pv
                sch.wait("scalar", "p", pv)
                sch.wait("scalar", "v", last_sil_v)
                sch.wait("scalar", "g", sch.now("g"))
                sv_ = sch.emit("scalar", lambda e, bka=bka: e.activation(tsil[:, :], P[bka][:, :], AF.Silu), inc="s")
                sch.wait("vector", "s", sv_)
                last_sil_v = sch.emit("vector", lambda e, ht2=ht2, bkb=bkb: e.tensor_tensor(
                    hid[:, ht2, :], tsil[:, :], P[bkb][:, :], ALU.mult), inc="v")
                bank_consumed(bka, "s", sv_)
                bank_consumed(bkb, "v", last_sil_v)
            hidf_done = last_sil_v
            sch.wait("tensor", "v", hidf_done)
            for bk in range(8):
                use_bank(bk)
            for hc in range(HT):
                buf = fdb[hc % 2]
                key = ("fdn", hc % 2)
                sem = "sf0" if hc % 2 == 0 else "sf1"
                sch.wait("sync", "p", stream_guard.get(key, 0))
                lv = dma("sync", sem, buf[:, :], wdbf[hc, :, :])
                sch.wait("tensor", sem, lv)
                for dt in range(8):
                    pv = sch.emit("tensor", lambda e, buf=buf, dt=dt, hc=hc: e.matmul(
                        P[dt][:, :], buf[:, dt * 128:(dt + 1) * 128], hid[:, hc, :],
                        start=(hc == 0), stop=(hc == HT - 1)), inc="p")
                stream_guard[key] = pv
            sch.wait("vector", "p", pv)
            for dt in range(8):
                vv = sch.emit("vector", lambda e, dt=dt: e.tensor_tensor(
                    lnx[:, dt, :], acc[:, dt, :], P[dt][:, :], ALU.add), inc="v")
                bank_consumed(dt, "v", vv)
            return pv

        def emit_ln3_out(k):
            nonlocal out_dma_mark
            emit_ln(None, ssm_mark.get(k + 1, 0))
            for di in range(NDI):
                sch.emit("vector", lambda e, di=di: e.tensor_tensor(tmp_f[:, :], lnx[:, di, :], P[4][:, :], ALU.mult))
                vv = sch.emit("vector", lambda e, di=di: e.tensor_tensor(acc[:, di, :], tmp_f[:, :], P[5][:, :], ALU.add), inc="v")
            bank_consumed(4, "v", vv); bank_consumed(5, "v", vv)
            sch.wait("scalar", "v", vv)
            out_dma_mark = dma("scalar", "qa", oR[:, :, k * MC:(k + 1) * MC], acc[:, :, :])

        vc = emit_load_cast(0)
        emit_delta(0, vc)
        for part in range(4):
            bgfinish(ssm_gen(0, part))
        emit_routing(0)
        for k in range(NMC):
            if k + 1 < NMC:
                vc = emit_load_cast(k + 1)
            emit_acc_init(k)
            if k + 1 < NMC:
                emit_delta(k + 1, vc)
            dn_prev = max(stream_guard.get(("fdn", 0), 0), stream_guard.get(("fdn", 1), 0))
            for ee in range(E):
                bg = ssm_gen(k + 1, ee) if k + 1 < NMC else None
                mid = (lambda k=k: emit_masks(k)) if ee == 0 else None
                dn_prev = emit_moe_expert(k, ee, dn_prev, bg, mid)
                bgfinish(bg)
            moe_mark[k] = sch.now("g")
            x1v = emit_lns(k)
            emit_ffn(k, x1v)
            if k + 1 < NMC:
                emit_routing(k + 1)
            emit_ln3_out(k)

        sch.wait("sync", "qg", sch.now("qg"))
        sch.wait("sync", "qa", sch.now("qa"))
        sch.wait("sync", "qs", sch.now("qs"))

        with nc.Block() as block:
            @block.sync
            def _(eng):
                for fn in sch.ops["sync"]:
                    fn(eng)

            @block.vector
            def _(eng):
                for fn in sch.ops["vector"]:
                    fn(eng)

            @block.scalar
            def _(eng):
                for fn in sch.ops["scalar"]:
                    fn(eng)

            @block.tensor
            def _(eng):
                for fn in sch.ops["tensor"]:
                    fn(eng)

            @block.gpsimd
            def _(eng):
                for fn in sch.ops["gpsimd"]:
                    fn(eng)

    return nc


def make_inputs_per_core(inputs, b):
    return {
        "xT": np.ascontiguousarray(inputs["L0_attn_x"][b].T),
        "A_log": np.asarray(inputs["L0_attn_A_log"]),
        "D_param": np.asarray(inputs["L0_attn_D_param"]),
        "Wd": np.asarray(inputs["L0_attn_Wd"]),
        "bd": np.asarray(inputs["L0_attn_bd"]),
        "WB": np.asarray(inputs["L0_attn_WB"]),
        "bB": np.asarray(inputs["L0_attn_bB"]),
        "WC": np.asarray(inputs["L0_attn_WC"]),
        "bC": np.asarray(inputs["L0_attn_bC"]),
        "Wr": np.asarray(inputs["L0_attn_Wr"]),
        "br": np.asarray(inputs["L0_attn_br"]),
        "up_w": np.asarray(inputs["L0_attn_up_w"]),
        "up_b": np.asarray(inputs["L0_attn_up_b"]),
        "down_w": np.asarray(inputs["L0_attn_down_w"]),
        "down_b": np.asarray(inputs["L0_attn_down_b"]),
        "ln_w": np.asarray(inputs["L0_attn_ln_w"]),
        "ln_b": np.asarray(inputs["L0_attn_ln_b"]),
        "wgT": np.ascontiguousarray(np.asarray(inputs["L0_ffn_w_gate"]).T),
        "wuT": np.ascontiguousarray(np.asarray(inputs["L0_ffn_w_up"]).T),
        "wdT": np.ascontiguousarray(np.asarray(inputs["L0_ffn_w_down"]).T),
        "ident": np.eye(128, dtype=np.float32),
    }


def kernel(**inputs):
    x = np.asarray(inputs["L0_attn_x"])
    B, L, _ = x.shape
    nc = build_program(L=L)
    in_maps = [make_inputs_per_core(inputs, b) for b in range(B)]
    res = run_bass_kernel_spmd(nc, in_maps, core_ids=list(range(B)))
    out = np.stack([np.ascontiguousarray(r["outT"].T) for r in res.results], axis=0)
    return out.astype(np.float32)
